# version 04
# baseline (speedup 1.0000x reference)
import numpy as np
import ml_dtypes

import concourse.bass as bass
import concourse.mybir as mybir
from concourse.bass_utils import run_bass_kernel_spmd

F32 = mybir.dt.float32
BF16 = mybir.dt.bfloat16
AF = mybir.ActivationFunctionType

D = 1024
SEQ = 2048
NT = SEQ // 128
NCH = D // 128
NHEAD = 16
KB = 1024

DEBUG = {"stage": None}
SAFE_MODE = {"on": False}
SAFE_TAGS = {"tags": set()}

_ENG2SEM = {"sync": "ld", "gpsimd": "g", "tensor": "t", "vector": "v", "scalar": "s"}


class Sched:
    ENGINES = ("sync", "gpsimd", "tensor", "vector", "scalar")

    def __init__(self):
        self.q = {e: [] for e in self.ENGINES}
        self.cnt = {"ld": 0, "g": 0, "t": 0, "v": 0, "s": 0}
        self.tag = "load"

    def emit(self, eng, fn, waits=(), inc=None):
        if SAFE_MODE["on"]:
            drop = SAFE_MODE.get("drop", ())
            dropeng = SAFE_MODE.get("dropeng")
            droptag = SAFE_MODE.get("droptag")
            dd = (droptag is None) or (self.tag == droptag)
            waits = tuple(waits) + tuple(
                (s_, self.cnt[s_]) for s_ in self.cnt
                if not (s_ in drop and (dropeng is None or eng == dropeng) and dd))
            if inc is None:
                inc = _ENG2SEM[eng]
        waits = tuple((sem, val) for (sem, val) in waits if val)
        self.q[eng].append((fn, waits, inc))
        if inc is not None:
            self.cnt[inc] += 16 if inc in ("ld", "g") else 1
            return self.cnt[inc]
        return None

    def build(self, block, sems):
        for ename in self.ENGINES:
            ops = list(self.q[ename])

            def make(ops):
                def run(engine):
                    for fn, waits, inc in ops:
                        for sem, val in waits:
                            engine.wait_ge(sems[sem], val)
                        ins = fn(engine)
                        if inc is not None:
                            ins.then_inc(sems[inc], 16 if inc in ("ld", "g") else 1)
                return run
            getattr(block, ename)(make(ops))


def build_program(debug_stage=None):
    nc = bass.Bass("TRN2", target_bir_lowering=False, detect_race_conditions=False)

    dx = nc.dram_tensor("x", [SEQ, D], F32, kind="ExternalInput").ap()
    dw = {}
    for nm in ("qwT", "kwT", "vwT", "owT", "wr", "wi", "chi", "clo", "shi", "slo"):
        dw[nm] = nc.dram_tensor(nm, [D, D], F32, kind="ExternalInput").ap()
    db = {}
    for nm in ("qb", "kb", "vb", "ob", "fb"):
        db[nm] = nc.dram_tensor(nm, [D], F32, kind="ExternalInput").ap()
    dident = nc.dram_tensor("identin", [128, 128], BF16, kind="ExternalInput").ap()
    dmask = nc.dram_tensor("maskin", [128, 384], BF16, kind="ExternalInput").ap()
    dout = nc.dram_tensor("out", [SEQ, D], F32, kind="ExternalOutput").ap()
    dmsp = nc.dram_tensor("mspill", [D, D], BF16, kind="Internal").ap()
    ddbg = None
    if debug_stage is not None:
        ddbg = nc.dram_tensor("dbg", [128, 16384], BF16, kind="ExternalOutput").ap()

    ARENA = 212352
    arena = nc.alloc_sbuf_tensor("arena", [128, ARENA], mybir.dt.uint8)
    base = nc.lookup_mloc(arena).addr
    cur = [0]

    def place(name, shape, dtype, at=None):
        nbytes = int(np.prod(shape[1:])) * mybir.dt.size(dtype)
        if at is None:
            at = cur[0]
        cur[0] = max(cur[0], at + ((nbytes + 31) // 32) * 32)
        h = nc.virtual_sbuf_tensor(name, list(shape), dtype)
        nc._tensor(h.name, h.shape, h.dtype, type="SB")
        mloc = nc.lookup_mloc(h)
        mloc.allocated = True
        mloc.addr = base + at
        assert at + nbytes <= ARENA, f"{name} overflows arena ({at + nbytes})"
        return h

    Xt = place("Xt", [128, NT * KB], BF16)
    identT = place("identT", [128, 128], BF16)
    maskc = place("maskcS", [128, 384], BF16)
    onescol = place("onescolB", [128, 1], BF16)
    onesf = place("onesfR", [1, 128], F32)
    cns = place("cns", [128, 8], F32)
    qb_sb = place("qb_sb", [128, 8], F32)
    kb_sb = place("kb_sb", [128, 8], F32)
    colbias = place("colbias", [128, KB], BF16)
    colbias2 = place("colbias2", [128, KB], BF16)
    st6a = place("st6a", [128, 12], F32)
    st6b = place("st6b", [128, 12], F32)
    st6c = place("st6c", [128, 12], F32)
    st6d = place("st6d", [128, 12], F32)
    st6p4 = [st6a, st6b, st6c, st6d]
    mvtp = [place("mvt%d" % i, [128, 2], F32) for i in range(4)]
    rrp = [place("rr%d" % i, [128, 1], F32) for i in range(NT)]
    bbp = [place("bb%d" % i, [128, 1], F32) for i in range(NT)]
    rcpt = place("rcpt", [128, 8], F32)

    R1 = cur[0]; place("R1pad", [128, 32 * KB], mybir.dt.uint8, at=R1)
    R2 = cur[0]; place("R2pad", [128, 32 * KB], mybir.dt.uint8, at=R2)
    R3 = cur[0]; place("R3pad", [128, 32 * KB], mybir.dt.uint8, at=R3)
    R4 = cur[0]; place("R4pad", [128, 36 * KB], mybir.dt.uint8, at=R4)
    R5 = cur[0]; place("R5pad", [128, 16 * KB], mybir.dt.uint8, at=R5)
    R6 = cur[0]; place("R6pad", [128, 16 * KB], mybir.dt.uint8, at=R6)
    R7 = cur[0]; place("R7pad", [128, 4 * KB], mybir.dt.uint8, at=R7)
    assert cur[0] <= ARENA, f"arena too small: {cur[0]}"

    ATr = place("ATr", [128, 8192], BF16, at=R1)
    ATi = place("ATi", [128, 8192], BF16, at=R1 + 16 * KB)
    ATboth = place("ATboth", [128, 16384], BF16, at=R1)
    QT = place("QT", [128, 16384], BF16, at=R1)
    zt = place("zt", [128, 4096], F32, at=R1)
    ut = place("ut", [128, 4096], BF16, at=R1 + 16 * KB)

    Cc = place("Cc", [128, 8192], BF16, at=R2)
    Scm = place("Scm", [128, 8192], BF16, at=R2 + 16 * KB)
    KT = place("KT", [128, 16384], BF16, at=R2)
    ostg = place("ostg", [128, 2048], F32, at=R2)

    Wrt = place("Wrt", [128, 8192], BF16, at=R3)
    Wit = place("Wit", [128, 8192], BF16, at=R3 + 16 * KB)
    Vt = place("Vt", [128, 16384], BF16, at=R3)

    tmpcs = place("tmpcs", [128, 8192], BF16, at=R4)
    hbT = place("hbT", [128, 16384], BF16, at=R4)
    aT = place("aT", [128, 16384], BF16, at=R4)
    h2T = place("h2T", [128, 16384], BF16, at=R4)
    hbtok = place("hbtok", [128, 2048], BF16, at=R4 + 32 * KB)
    h2tok = place("h2tok", [128, 2048], BF16, at=R4 + 32 * KB)

    Snm = place("Snm", [128, 8192], BF16, at=R4 + 16 * KB)
    Mt = place("Mt", [128, 8192], BF16, at=R5)
    wslotB = place("wslotB", [128, 8192], BF16, at=R5)
    PTr = place("PTr", [128, 4 * 384], BF16, at=R5)
    apair = place("apair", [128, 4096], BF16, at=R5 + 4 * KB)

    wslot = place("wslot", [128, 8192], BF16, at=R6)
    rowstg = place("rowstg", [1, 1024], F32, at=R7)
    rrp2 = [place("rr2_%d" % i, [128, 1], F32, at=R7 + i * 32) for i in range(NT)]
    bbp2 = [place("bb2_%d" % i, [128, 1], F32, at=R7 + 512 + i * 32) for i in range(NT)]

    pbig = nc.alloc_psum_tensor("pbig", [128, 3072], F32)
    ptrp = nc.alloc_psum_tensor("ptrp", [128, 2048], BF16)

    sems = {s_: nc.alloc_semaphore(name="sem_" + s_) for s_ in ("ld", "g", "t", "v", "s")}
    sc = Sched()
    dbg_gate = [0]

    def seg(name):
        sc.tag = name
        if SAFE_TAGS["tags"]:
            SAFE_MODE["on"] = (name in SAFE_TAGS["tags"]) or ("all" in SAFE_TAGS["tags"])

    def dbg_dump(name, src, waits):
        if debug_stage != name:
            return
        n = src.shape[1]
        m = sc.emit("gpsimd", lambda e, src=src, n=n: e.dma_start(out=ddbg[:, 0:n], in_=src[:, 0:n]),
                    waits=waits, inc="g")
        dbg_gate[0] = m

    def dma(q, out, in_, waits=(), slow=False):
        if slow:
            return sc.emit(q, lambda e, out=out, in_=in_: e.dma_start(
                out=out, in_=in_, allow_slow_non_contiguous=True), waits=waits,
                inc={"sync": "ld", "gpsimd": "g"}[q])
        return sc.emit(q, lambda e, out=out, in_=in_: e.dma_start(out=out, in_=in_),
                       waits=waits, inc={"sync": "ld", "gpsimd": "g"}[q])

    def T(fn, waits=(), inc=None):
        return sc.emit("tensor", fn, waits, inc)

    def Vx(fn, waits=(), inc=None):
        return sc.emit("vector", fn, waits, inc)

    def Sx(fn, waits=(), inc=None):
        return sc.emit("scalar", fn, waits, inc)

    bank = [pbig[:, b * 512:(b + 1) * 512] for b in range(6)]

    def mm(out_ap, lh, rh, start, stop, waits=(), inc=None):
        return T(lambda e, out_ap=out_ap, lh=lh, rh=rh, start=start, stop=stop:
                 e.matmul(out_ap, lh, rh, start=start, stop=stop, skip_group_check=True),
                 waits=waits, inc=inc)

    seg("load")
    Vx(lambda e: e.memset(cns[:, 0:1], 1e-5))
    Vx(lambda e: e.memset(cns[:, 1:2], -0.5))
    Vx(lambda e: e.memset(cns[:, 2:3], 0.125))
    Vx(lambda e: e.memset(cns[:, 3:4], 1.0 / 1024.0))
    Vx(lambda e: e.memset(cns[:, 4:5], -1.0 / 1024.0))
    Vx(lambda e: e.memset(cns[:, 5:6], -1.0))
    Vx(lambda e: e.memset(onescol[:], 1.0))
    init_v = Vx(lambda e: e.memset(onesf[:], 1.0), inc="v")

    ld_ident = dma("sync", identT[:], dident)
    ld_mask = dma("sync", maskc[:], dmask)
    ld_qb = dma("sync", qb_sb[:], db["qb"].rearrange("(c p) -> p c", p=128), slow=True)
    ld_kb = dma("sync", kb_sb[:], db["kb"].rearrange("(c p) -> p c", p=128), slow=True)

    def castdma(dst_t, src_ap, nchunk, waits=()):
        dst = dst_t[:, :].rearrange("p (c j) -> p c j", c=nchunk)
        srcv = src_ap.rearrange("(c p) j -> p c j", p=128)
        return dma("gpsimd", dst, srcv, waits=waits)

    dma("gpsimd", Xt[:, 0:8192].rearrange("p (c j) -> p c j", c=8),
        dx[0:1024, :].rearrange("(c p) j -> p c j", p=128))
    x_g = dma("gpsimd", Xt[:, 8192:16384].rearrange("p (c j) -> p c j", c=8),
              dx[1024:2048, :].rearrange("(c p) j -> p c j", p=128))
    qw_g = castdma(wslot, dw["qwT"], NCH)
    castdma(Cc, dw["chi"], NCH)
    cc_lo = castdma(tmpcs, dw["clo"], NCH)
    cadd = None
    for jc in range(NCH):
        cadd = Vx(lambda e, jc=jc: e.tensor_add(
            Cc[:, jc * KB:(jc + 1) * KB], tmpcs[:, jc * KB:(jc + 1) * KB],
            Cc[:, jc * KB:(jc + 1) * KB]),
            waits=(("g", cc_lo), ("v", init_v)) if jc == 0 else (), inc="v")
    castdma(Scm, dw["shi"], NCH, waits=(("v", cadd),))
    ss_lo = castdma(tmpcs, dw["slo"], NCH)
    sadd = None
    for jc in range(NCH):
        sadd = Vx(lambda e, jc=jc: e.tensor_add(
            Scm[:, jc * KB:(jc + 1) * KB], tmpcs[:, jc * KB:(jc + 1) * KB],
            Scm[:, jc * KB:(jc + 1) * KB]),
            waits=(("g", ss_lo),) if jc == 0 else (), inc="v")
    sn_m = None
    for jc in range(NCH):
        sn_m = Vx(lambda e, jc=jc: e.tensor_scalar_mul(
            out=Snm[:, jc * KB:(jc + 1) * KB], in0=Scm[:, jc * KB:(jc + 1) * KB],
            scalar1=cns[:, 5:6]),
            waits=(("v", sadd),) if jc == 0 else (), inc="v")
    castdma(Wrt, dw["wr"], NCH)
    wi_g = castdma(Wit, dw["wi"], NCH)
    seg("p0")
    evac_free = [("s", 0)] * 6

    def accum(mms, out_ap, b, evac_fn, eng="scalar", first_waits=(), evac_waits=()):
        n = len(mms)
        tm = None
        for i, (lh, rh) in enumerate(mms):
            w = (list(first_waits) + [evac_free[b]]) if i == 0 else []
            tm = mm(out_ap, lh, rh, i == 0, i == n - 1, waits=w,
                    inc=("t" if i == n - 1 else None))
        ew = [("t", tm)] + list(evac_waits)
        if eng == "scalar":
            em = Sx(evac_fn, waits=ew, inc="s")
            evac_free[b] = ("s", em)
        else:
            em = Vx(evac_fn, waits=ew, inc="v")
            evac_free[b] = ("v", em)
        return em

    bank_it = [0]

    def nextb(ring=(0, 1, 2, 3)):
        b = ring[bank_it[0] % len(ring)]
        bank_it[0] += 1
        return b

    def wch(t, f, blk):
        return t[:, f * KB + blk * 128: f * KB + (blk + 1) * 128]

    def cch(t, f, w):
        return t[:, f * KB + w * 512: f * KB + (w + 1) * 512]

    first = True
    at_evac = {}
    for dst, pA, pB, sgn in ((ATr, (Wrt, Cc), (Wit, Scm), 3), (ATi, (Wit, Cc), (Wrt, Snm), 4)):
        for oc in range(NCH):
            for dwin in range(2):
                b = nextb()
                mms = [(wch(pA[0], f, oc), cch(pA[1], f, dwin)) for f in range(NCH)] + \
                      [(wch(pB[0], f, oc), cch(pB[1], f, dwin)) for f in range(NCH)]

                def ev(e, dst=dst, oc=oc, dwin=dwin, b=b, sgn=sgn):
                    return e.activation(
                        out=dst[:, oc * KB + dwin * 512: oc * KB + (dwin + 1) * 512],
                        in_=bank[b], func=AF.Identity, bias=0.0, scale=cns[:, sgn:sgn + 1])
                fw = (("g", wi_g), ("v", sn_m)) if first else ()
                first = False
                at_evac[(dst is ATi, oc, dwin)] = accum(mms, bank[b], b, ev, first_waits=fw)
    at_done = sc.cnt["s"]

    m_evac = {}
    for dc in range(NCH):
        for tw in range(2):
            b = nextb()
            mms = [(ATr[:, oc * KB + dc * 128: oc * KB + (dc + 1) * 128], cch(Cc, oc, tw))
                   for oc in range(NCH)] + \
                  [(ATi[:, oc * KB + dc * 128: oc * KB + (dc + 1) * 128], cch(Scm, oc, tw))
                   for oc in range(NCH)]

            def ev(e, dc=dc, tw=tw, b=b):
                return e.activation(out=Mt[:, dc * KB + tw * 512: dc * KB + (tw + 1) * 512],
                                    in_=bank[b], func=AF.Identity, bias=0.0, scale=1.0)
            fw = (("s", at_done),) if (dc == 0 and tw == 0) else ()
            m_evac[(dc, tw)] = accum(mms, bank[b], b, ev, first_waits=fw)

    m_spill_m = None
    for dc in range(NCH):
        m_spill_m = dma("gpsimd", dmsp[dc * 128:(dc + 1) * 128, :], Mt[:, dc * KB:(dc + 1) * KB],
                        waits=(("s", m_evac[(dc, 1)]),))
    if debug_stage == "AT":
        sc.emit("gpsimd", lambda e: e.dma_start(out=ddbg[:, 0:8192], in_=ATr[:, :]),
                waits=(("s", at_done),), inc="g")
        dbg_gate[0] = sc.emit("gpsimd", lambda e: e.dma_start(
            out=ddbg[:, 8192:16384], in_=ATi[:, :]), waits=(), inc="g")

    def build_colbias(nm, row_wait_s=0, mm_wait_v=0, evac_wait_v=0, dst=None):
        if dst is None:
            dst = colbias
        wl = [("s", row_wait_s)] if row_wait_s else []
        mrow = dma("sync", rowstg[0:1, :], db[nm].rearrange("(a b) -> a b", a=1), waits=wl)
        out_m = []
        for wnd in range(2):
            b = 4 + wnd
            tm = mm(bank[b], onesf[:, :], rowstg[0:1, wnd * 512:(wnd + 1) * 512],
                    True, True, waits=[("ld", mrow), ("v", init_v), evac_free[b],
                                       ("v", mm_wait_v)], inc="t")
            em = Sx(lambda e, wnd=wnd, b=b, dst=dst: e.activation(
                out=dst[:, wnd * 512:(wnd + 1) * 512], in_=bank[b],
                func=AF.Identity, bias=0.0, scale=1.0),
                waits=(("t", tm), ("v", evac_wait_v)), inc="s")
            evac_free[b] = ("s", em)
            out_m.append(em)
        return out_m[-1]

    vb_ready = build_colbias("vb")

    stats_gate = [0, 0, 0, 0]
    apply_m16 = [0] * NT
    ln_tok_gate = [0, 0]
    ptr_free = [("v", 0), ("v", 0)]

    def ln_stats(src_ap, nt, extra_waits=(), par=None, rarr=None, barr=None):
        if par is None:
            par = nt % 4
        if rarr is None:
            rarr, barr = rrp, bbp
        w = [("s", stats_gate[par])] + list(extra_waits)
        Vx(lambda e: e.bn_stats(out=st6p4[par][:, 0:6], in_=src_ap[:, 0:512]), waits=w)
        b2 = Vx(lambda e: e.bn_stats(out=st6p4[par][:, 6:12], in_=src_ap[:, 512:1024]),
                inc="v")
        mv = Vx(lambda e: e.bn_aggr(out=mvtp[par][:, :], in_=st6p4[par][:, :]),
                waits=(("v", b2),), inc="v")
        c1 = Sx(lambda e: e.activation(out=rarr[nt][:, :], in_=mvtp[par][:, 1:2],
                                       func=AF.Ln, bias=cns[:, 0:1], scale=1.0),
                waits=(("v", mv),), inc="s")
        c2 = Sx(lambda e: e.activation(out=rarr[nt][:, :], in_=rarr[nt][:, :],
                                       func=AF.Exp, bias=0.0, scale=cns[:, 1:2]),
                waits=(("s", c1),), inc="s")
        c3 = Sx(lambda e: e.mul(out=barr[nt][:, :], in_=rarr[nt][:, :],
                                mul=mvtp[par][:, 0:1]),
                waits=(("s", c2),), inc="s")
        c4 = Sx(lambda e: e.mul(out=barr[nt][:, :], in_=barr[nt][:, :], mul=cns[:, 5:6]),
                waits=(("s", c3),), inc="s")
        stats_gate[par] = c4
        return c4

    def ln_apply(src_ap, tok_out, nt, chain_m, tok_gate_t=0, rarr=None, barr=None):
        if rarr is None:
            rarr, barr = rrp, bbp
        am = Sx(lambda e, rarr=rarr, barr=barr: e.activation(
            out=tok_out, in_=src_ap, func=AF.Identity,
            bias=barr[nt][:, :], scale=rarr[nt][:, :]),
            waits=(("s", chain_m), ("t", tok_gate_t)), inc="s")
        apply_m16[nt] = am
        return am

    ident_waited = [False]

    def emit_transposes(tok_ap, dstT, nt, apply_m, extra_w=None):
        pb = nt % 2
        tm = None
        for jc in range(NCH):
            w = [("s", apply_m), ptr_free[pb]] if jc == 0 else []
            if jc == 0 and extra_w:
                w.append(extra_w)
            if jc == 0 and not ident_waited[0]:
                w.append(("ld", ld_ident))
                ident_waited[0] = True
            tm = T(lambda e, jc=jc, pb=pb, tok_ap=tok_ap: e.transpose(
                ptrp[:, pb * 1024 + jc * 128: pb * 1024 + (jc + 1) * 128],
                tok_ap[:, jc * 128:(jc + 1) * 128], identT[:]),
                waits=w, inc=("t" if jc == NCH - 1 else None))
        vm = None
        for g in range(2):
            srcv = ptrp[:, pb * 1024 + g * 512: pb * 1024 + (g + 1) * 512].rearrange(
                "p (a b) -> p a b", a=4)
            dstv = dstT[:, :].rearrange("p (j n) -> p j n", j=NCH)[
                :, 4 * g:4 * g + 4, nt * 128:(nt + 1) * 128]
            vm = Vx(lambda e, srcv=srcv, dstv=dstv: e.tensor_copy(out=dstv, in_=srcv),
                    waits=(("t", tm),) if g == 0 else (), inc=("v" if g == 1 else None))
        ptr_free[pb] = ("v", vm)
        return tm, vm

    hbT_done = None
    seg("ln1")
    ln1_chain = [None] * NT
    for nt in range(NT):
        ln1_chain[nt] = ln_stats(Xt[:, nt * KB:(nt + 1) * KB], nt,
                                 extra_waits=((("g", x_g),) if nt == 0 else ()))
    for nt in range(NT):
        am = ln_apply(Xt[:, nt * KB:(nt + 1) * KB],
                      hbtok[:, (nt % 2) * 1024:(nt % 2 + 1) * 1024], nt,
                      ln1_chain[nt], tok_gate_t=ln_tok_gate[nt % 2])
        tm, vm = emit_transposes(hbtok[:, (nt % 2) * 1024:(nt % 2 + 1) * 1024], hbT, nt, am)
        ln_tok_gate[nt % 2] = tm
        hbT_done = vm
    dbg_dump("hbT", hbT, (("v", hbT_done),))

    seg("qkv")
    RING6 = (0, 1, 2, 3, 4, 5)

    def proj_featmajor(dstT, bias_sb, wsl, w_g):
        for ic in range(NCH):
            for nw in range(4):
                b = nextb(RING6)
                mms = [(wsl[:, jc * KB + ic * 128: jc * KB + (ic + 1) * 128],
                        hbT[:, jc * 2048 + nw * 512: jc * 2048 + (nw + 1) * 512])
                       for jc in range(NCH)]

                def ev(e, ic=ic, nw=nw, b=b, dstT=dstT, bias_sb=bias_sb):
                    return e.activation(out=dstT[:, ic * 2048 + nw * 512: ic * 2048 + (nw + 1) * 512],
                                        in_=bank[b], func=AF.Identity,
                                        bias=bias_sb[:, ic:ic + 1], scale=1.0)
                fw = [("v", hbT_done), ("g", w_g)] if (ic == 0 and nw == 0) else []
                if ic == 0 and nw == 0 and debug_stage == "AT" and dstT is QT:
                    fw.append(("g", dbg_gate[0]))
                ew_b = [("ld", max(ld_qb, ld_kb))] if (ic == 0 and nw == 0) else []
                accum(mms, bank[b], b, ev, first_waits=fw, evac_waits=ew_b)
        return sc.cnt["t"], sc.cnt["s"]

    q_t_done, q_s_done = proj_featmajor(QT, qb_sb, wslot, qw_g)
    kw_g = castdma(wslotB, dw["kwT"], NCH, waits=(("t", q_t_done),))
    k_t_done, k_s_done = proj_featmajor(KT, kb_sb, wslotB, kw_g)
    vw_g = castdma(wslot, dw["vwT"], NCH, waits=(("t", q_t_done),))

    v_t_done = None
    v_v_done = None
    for nt in range(NT):
        for iw in range(2):
            b = nextb(RING6)
            mms = [(hbT[:, jc * 2048 + nt * 128: jc * 2048 + (nt + 1) * 128],
                    wslot[:, jc * KB + iw * 512: jc * KB + (iw + 1) * 512])
                   for jc in range(NCH)]

            def ev(e, nt=nt, iw=iw, b=b):
                return e.tensor_add(Vt[:, nt * KB + iw * 512: nt * KB + (iw + 1) * 512],
                                    bank[b], colbias[:, iw * 512:(iw + 1) * 512])
            fw = [("g", vw_g), ("s", vb_ready)] if (nt == 0 and iw == 0) else []
            v_v_done = accum(mms, bank[b], b, ev, eng="vector", first_waits=fw)
    v_t_done = sc.cnt["t"]
    ow_g = castdma(wslot, dw["owT"], NCH, waits=(("t", v_t_done),))
    dbg_dump("QT", QT, (("s", q_s_done),))
    dbg_dump("KT", KT, (("s", k_s_done),))
    dbg_dump("V", Vt, (("v", v_v_done),))

    seg("a2")
    NJ = [384] * 14 + [256, 128]
    exp_bank = [0, 0]
    av_free = [("v", 0), ("v", 0)]
    aT_done = None
    qk_gate = max(q_s_done, k_s_done)
    first_score = True
    for h in range(NHEAD):
        hp = h // 2
        po = (h % 2) * 64
        pbuf = (h // 2) % 2
        for j in range(NT):
            nj = NJ[j]
            sb_ = j % 2
            w = [("s", exp_bank[sb_])]
            if first_score:
                w.append(("s", qk_gate))
                w.append(("v", v_v_done))
                w.append(("ld", ld_mask))
                first_score = False
            mm(pbig[:, sb_ * 512: sb_ * 512 + nj],
               KT[po:po + 64, hp * 2048 + j * 128: hp * 2048 + (j + 1) * 128],
               QT[po:po + 64, hp * 2048 + j * 128: hp * 2048 + j * 128 + nj],
               True, False, waits=w)
            sm = mm(pbig[:, sb_ * 512: sb_ * 512 + nj], identT[:], maskc[:, 0:nj],
                    False, True, inc="t")
            ew_ = [("t", sm)]
            if h == 0 and j == 0:
                ew_.append(("g", m_spill_m))
            em = Sx(lambda e, sb_=sb_, nj=nj, j=j: e.activation(
                out=PTr[:, (j % 4) * 384: (j % 4) * 384 + nj],
                in_=pbig[:, sb_ * 512: sb_ * 512 + nj],
                func=AF.Exp, bias=0.0, scale=cns[:, 2:3]), waits=ew_, inc="s")
            exp_bank[sb_] = em
            qtg = h * NT + j
            slot = qtg % 8
            pair = slot // 4
            dcol = (2 + pair) * 512 + (slot % 4) * 128
            scol = (4 + pair) * 512 + (slot % 4) * 128
            djs = list(range(max(0, j - 2), j + 1))
            avm = None
            for i, dj in enumerate(djs):
                lh = PTr[:, (dj % 4) * 384 + (j - dj) * 128: (dj % 4) * 384 + (j - dj + 1) * 128]
                rh = Vt[:, dj * KB + h * 64: dj * KB + h * 64 + 64]
                w = [("s", em), av_free[pair]] if i == 0 else []
                if i == 0 and h == 0 and j == 0:
                    w.append(("v", v_v_done))
                mm(pbig[:, dcol:dcol + 64], lh, rh, i == 0, i == len(djs) - 1, waits=w)
                avm = mm(pbig[:, scol:scol + 1], lh, onescol[:], i == 0, i == len(djs) - 1,
                         inc=("t" if i == len(djs) - 1 else None))
            if slot % 4 == 3:
                vm = None
                for s4 in range(4):
                    sl = pair * 4 + s4
                    qj = j - 3 + s4
                    dc4 = (2 + pair) * 512 + s4 * 128
                    sc4 = (4 + pair) * 512 + s4 * 128
                    rm = Vx(lambda e, sl=sl, sc4=sc4: e.reciprocal(
                        out=rcpt[:, sl:sl + 1], in_=pbig[:, sc4:sc4 + 1]),
                        waits=(("t", avm),) if s4 == 0 else (), inc="v")
                    vm = Vx(lambda e, sl=sl, dc4=dc4, pbuf=pbuf, qj=qj, po=po: e.tensor_scalar_mul(
                        out=apair[:, pbuf * 2048 + qj * 128 + po: pbuf * 2048 + qj * 128 + po + 64],
                        in0=pbig[:, dc4:dc4 + 64], scalar1=rcpt[:, sl:sl + 1]),
                        waits=(("v", rm),), inc=("v" if s4 == 3 else None))
                av_free[pair] = ("v", vm)
        if h % 2 == 1:
            c = h // 2
            tm = None
            for qt in range(NT):
                pb = qt // 8
                w = [av_free[0], av_free[1]] if qt == 0 else []
                if qt % 8 == 0:
                    w.append(ptr_free[pb])
                tm = T(lambda e, qt=qt, pb=pb, pbuf=pbuf: e.transpose(
                    ptrp[:, (qt % 16) * 128: (qt % 16 + 1) * 128],
                    apair[:, pbuf * 2048 + qt * 128: pbuf * 2048 + (qt + 1) * 128],
                    identT[:]), waits=w, inc=("t" if qt % 8 == 7 else None))
                if qt % 8 == 7:
                    wv = [("t", tm)]
                    if debug_stage == "hbT":
                        wv.append(("g", dbg_gate[0]))
                    vm = Vx(lambda e, pb=pb, c=c: e.tensor_copy(
                        out=aT[:, c * 2048 + pb * 1024: c * 2048 + (pb + 1) * 1024],
                        in_=ptrp[:, pb * 1024:(pb + 1) * 1024]),
                        waits=wv, inc="v")
                    ptr_free[pb] = ("v", vm)
                    aT_done = vm

    seg("a3")
    ob_ready = build_colbias("ob", row_wait_s=vb_ready, mm_wait_v=aT_done,
                             evac_wait_v=v_v_done)
    fb_ready = build_colbias("fb", row_wait_s=ob_ready, dst=colbias2)
    x1_m = [None] * NT
    ln3_chain = [None] * NT
    for nt in range(NT):
        par = nt % 2
        zms = []
        for iw in range(2):
            b = nextb(RING6)
            mms = [(aT[:, dc * 2048 + nt * 128: dc * 2048 + (nt + 1) * 128],
                    wslot[:, dc * KB + iw * 512: dc * KB + (iw + 1) * 512])
                   for dc in range(NCH)]

            def ev(e, nt=nt, iw=iw, b=b, par=par):
                return e.tensor_add(zt[:, par * 1024 + iw * 512: par * 1024 + (iw + 1) * 512],
                                    bank[b], colbias[:, iw * 512:(iw + 1) * 512])
            fw = []
            if nt == 0 and iw == 0:
                fw = [("g", ow_g), ("v", aT_done), ("s", ob_ready)]
            ew = [("s", apply_m16[nt - 2] if nt >= 2 else 0)] if iw == 0 else []
            if nt == 0 and iw == 0 and debug_stage in ("QT", "KT"):
                ew.append(("g", dbg_gate[0]))
            zms.append(accum(mms, bank[b], b, ev, eng="vector", first_waits=fw,
                             evac_waits=ew))
        cm4 = ln_stats(zt[:, par * 1024:(par + 1) * 1024], nt,
                       extra_waits=(("v", zms[-1]),), par=par)
        am = ln_apply(zt[:, par * 1024:(par + 1) * 1024],
                      ut[:, par * 1024:(par + 1) * 1024], nt, cm4)
        xm = Vx(lambda e, nt=nt, par=par: e.tensor_add(
            Xt[:, nt * KB:(nt + 1) * KB], Xt[:, nt * KB:(nt + 1) * KB],
            ut[:, par * 1024:(par + 1) * 1024]), waits=(("s", am),), inc="v")
        x1_m[nt] = xm
        ew3 = [("v", xm), ("s", fb_ready)]
        ln3_chain[nt] = ln_stats(Xt[:, nt * KB:(nt + 1) * KB], nt,
                                 extra_waits=ew3, par=2 + nt % 2,
                                 rarr=rrp2, barr=bbp2)
    a3_t_done = sc.cnt["t"]

    seg("f")
    m_rd = [None] * NCH
    for dc in range(NCH):
        w = [("t", a3_t_done), ("g", m_spill_m)] if dc == 0 else []
        m_rd[dc] = dma("sync", Mt[:, dc * KB:(dc + 1) * KB], dmsp[dc * 128:(dc + 1) * 128, :],
                       waits=w)
    dbg_dump("aT", aT, (("v", aT_done),))
    dbg_dump("x1", Xt, (("v", x1_m[-1]),))
    h2_done = None
    for nt in range(NT):
        gw = [("g", dbg_gate[0])] if (nt == 0 and debug_stage in ("aT", "x1")) else []
        am = ln_apply(Xt[:, nt * KB:(nt + 1) * KB],
                      h2tok[:, (nt % 2) * 1024:(nt % 2 + 1) * 1024], nt,
                      ln3_chain[nt], tok_gate_t=ln_tok_gate[nt % 2],
                      rarr=rrp2, barr=bbp2)
        tm, vm = emit_transposes(h2tok[:, (nt % 2) * 1024:(nt % 2 + 1) * 1024], h2T, nt, am,
                                 extra_w=(gw[0] if gw else None))
        ln_tok_gate[nt % 2] = tm
        h2_done = vm

    x2_m = [None] * NT
    for nt in range(NT):
        par = nt % 2
        yms = []
        for tw in range(2):
            b = nextb(RING6)
            mms = [(h2T[:, dc * 2048 + nt * 128: dc * 2048 + (nt + 1) * 128],
                    Mt[:, dc * KB + tw * 512: dc * KB + (tw + 1) * 512])
                   for dc in range(NCH)]

            def ev(e, nt=nt, tw=tw, b=b, par=par):
                return e.tensor_add(zt[:, par * 1024 + tw * 512: par * 1024 + (tw + 1) * 512],
                                    bank[b], colbias2[:, tw * 512:(tw + 1) * 512])
            fw = []
            if nt == 0 and tw == 0:
                fw = [("v", h2_done), ("ld", m_rd[-1]), ("s", fb_ready)]
            ew = [("s", apply_m16[nt - 2] if nt >= 2 else 0)] if tw == 0 else []
            yms.append(accum(mms, bank[b], b, ev, eng="vector", first_waits=fw,
                             evac_waits=ew))
        cm4 = ln_stats(zt[:, par * 1024:(par + 1) * 1024], nt,
                       extra_waits=(("v", yms[-1]),), par=par)
        am = ln_apply(zt[:, par * 1024:(par + 1) * 1024],
                      ut[:, par * 1024:(par + 1) * 1024], nt, cm4)
        xm = Vx(lambda e, nt=nt, par=par: e.tensor_add(
            Xt[:, nt * KB:(nt + 1) * KB], Xt[:, nt * KB:(nt + 1) * KB],
            ut[:, par * 1024:(par + 1) * 1024]), waits=(("s", am),), inc="v")
        x2_m[nt] = xm

    seg("outp")
    st_m = [0] * NT
    for nt in range(NT):
        par = nt % 2
        w = [("v", x2_m[nt])]
        if nt >= 2:
            w.append(("g", st_m[nt - 2]))
        cm = Sx(lambda e, nt=nt, par=par: e.copy(
            out=ostg[:, par * 1024:(par + 1) * 1024], in_=Xt[:, nt * KB:(nt + 1) * KB]),
            waits=w, inc="s")
        st_m[nt] = dma("gpsimd", dout[nt * 128:(nt + 1) * 128, :],
                       ostg[:, par * 1024:(par + 1) * 1024], waits=(("s", cm),))

    if debug_stage == "M":
        dma("gpsimd", ddbg[:, 0:8192], Mt[:, 0:8192],
            waits=(("v", sc.cnt["v"]), ("s", sc.cnt["s"]), ("t", sc.cnt["t"])))

    sc.emit("gpsimd", lambda e: e.engine_nop(), waits=(("g", sc.cnt["g"]),))

    with nc.Block() as block:
        sc.build(block, sems)
    return nc


def _host_constants():
    n = np.arange(D, dtype=np.float64)
    ang = (2.0 * np.pi / D) * np.outer(n, n)
    c64, s64 = np.cos(ang), np.sin(ang)
    chi = c64.astype(np.float32)
    clo = (c64 - chi).astype(np.float32)
    shi = s64.astype(np.float32)
    slo = (s64 - shi).astype(np.float32)
    ident = np.eye(128, dtype=ml_dtypes.bfloat16)
    kk = np.arange(128)[:, None]
    qq = np.arange(384)[None, :]
    dqk = qq - kk
    mask = np.where((dqk >= 0) & (dqk <= 255), 0.0, -30000.0).astype(ml_dtypes.bfloat16)
    return chi, clo, shi, slo, ident, mask


_NC_CACHE = {}


def kernel(L0_attn_x, L0_attn_q_w, L0_attn_q_b, L0_attn_k_w, L0_attn_k_b,
           L0_attn_v_w, L0_attn_v_b, L0_attn_o_w, L0_attn_o_b,
           L0_ffn_w_freq_real, L0_ffn_w_freq_imag, L0_ffn_bias):
    chi, clo, shi, slo, ident, mask = _host_constants()
    shared = {
        "qwT": np.ascontiguousarray(np.asarray(L0_attn_q_w, np.float32).T),
        "kwT": np.ascontiguousarray(np.asarray(L0_attn_k_w, np.float32).T),
        "vwT": np.ascontiguousarray(np.asarray(L0_attn_v_w, np.float32).T),
        "owT": np.ascontiguousarray(np.asarray(L0_attn_o_w, np.float32).T),
        "wr": np.ascontiguousarray(np.asarray(L0_ffn_w_freq_real, np.float32)),
        "wi": np.ascontiguousarray(np.asarray(L0_ffn_w_freq_imag, np.float32)),
        "chi": chi, "clo": clo, "shi": shi, "slo": slo,
        "qb": np.asarray(L0_attn_q_b, np.float32),
        "kb": np.asarray(L0_attn_k_b, np.float32),
        "vb": np.asarray(L0_attn_v_b, np.float32),
        "ob": np.asarray(L0_attn_o_b, np.float32),
        "fb": np.asarray(L0_ffn_bias, np.float32),
        "identin": ident, "maskin": mask,
    }
    x = np.asarray(L0_attn_x, np.float32)
    stage = DEBUG.get("stage")
    key = ("prog", stage)
    if key not in _NC_CACHE:
        _NC_CACHE[key] = build_program(stage)
    nc = _NC_CACHE[key]
    in_maps = []
    for b in range(8):
        m = dict(shared)
        m["x"] = np.ascontiguousarray(x[b])
        in_maps.append(m)
    res = run_bass_kernel_spmd(nc, in_maps, core_ids=list(range(8)))
    out = np.stack([np.asarray(r["out"], np.float32) for r in res.results], axis=0)
    if stage is not None:
        DEBUG["result"] = [np.asarray(r["dbg"]) for r in res.results]
    return out


# version 08
# speedup vs baseline: 1.2749x; 1.2749x over previous
import numpy as np
import ml_dtypes

import concourse.bass as bass
import concourse.mybir as mybir
from concourse.bass_utils import run_bass_kernel_spmd

F32 = mybir.dt.float32
BF16 = mybir.dt.bfloat16
AF = mybir.ActivationFunctionType

D = 1024
SEQ = 2048
NT = SEQ // 128
NCH = D // 128
NHEAD = 16
KB = 1024

DEBUG = {"stage": None}
SAFE_MODE = {"on": False}
SAFE_TAGS = {"tags": set()}

_ENG2SEM = {"sync": "ld", "gpsimd": "g", "tensor": "t", "vector": "v", "scalar": "s"}


class Sched:
    ENGINES = ("sync", "gpsimd", "tensor", "vector", "scalar")

    def __init__(self):
        self.q = {e: [] for e in self.ENGINES}
        self.cnt = {"ld": 0, "g": 0, "t": 0, "v": 0, "s": 0}
        self.tag = "load"

    def emit(self, eng, fn, waits=(), inc=None):
        if SAFE_MODE["on"]:
            drop = SAFE_MODE.get("drop", ())
            dropeng = SAFE_MODE.get("dropeng")
            droptag = SAFE_MODE.get("droptag")
            dd = (droptag is None) or (self.tag == droptag)
            waits = tuple(waits) + tuple(
                (s_, self.cnt[s_]) for s_ in self.cnt
                if not (s_ in drop and (dropeng is None or eng == dropeng) and dd))
            if inc is None:
                inc = _ENG2SEM[eng]
        waits = tuple((sem, val) for (sem, val) in waits if val)
        self.q[eng].append((fn, waits, inc))
        if inc is not None:
            self.cnt[inc] += 16 if inc in ("ld", "g") else 1
            return self.cnt[inc]
        return None

    def build(self, block, sems):
        for ename in self.ENGINES:
            ops = list(self.q[ename])

            def make(ops):
                def run(engine):
                    for fn, waits, inc in ops:
                        for sem, val in waits:
                            engine.wait_ge(sems[sem], val)
                        ins = fn(engine)
                        if inc is not None:
                            ins.then_inc(sems[inc], 16 if inc in ("ld", "g") else 1)
                return run
            getattr(block, ename)(make(ops))


def build_program(debug_stage=None):
    nc = bass.Bass("TRN2", target_bir_lowering=False, detect_race_conditions=False)

    dx = nc.dram_tensor("x", [SEQ, D], F32, kind="ExternalInput").ap()
    dw = {}
    for nm in ("qwT", "kwT", "vwT", "owT", "wr", "wi"):
        dw[nm] = nc.dram_tensor(nm, [D, D], F32, kind="ExternalInput").ap()
    for nm in ("c0", "c1", "c2", "s0", "s1", "s2"):
        dw[nm] = nc.dram_tensor(nm, [D, D], BF16, kind="ExternalInput").ap()
    db = {}
    for nm in ("qb", "kb", "vb", "ob", "fb"):
        db[nm] = nc.dram_tensor(nm, [D], F32, kind="ExternalInput").ap()
    dident = nc.dram_tensor("identin", [128, 128], BF16, kind="ExternalInput").ap()
    dmask = nc.dram_tensor("maskin", [128, 384], BF16, kind="ExternalInput").ap()
    dout = nc.dram_tensor("out", [SEQ, D], F32, kind="ExternalOutput").ap()
    dmsp = nc.dram_tensor("mspill", [D, D], BF16, kind="Internal").ap()
    ddbg = None
    if debug_stage is not None:
        ddbg = nc.dram_tensor("dbg", [128, 16384], BF16, kind="ExternalOutput").ap()

    ARENA = 212832
    arena = nc.alloc_sbuf_tensor("arena", [128, ARENA], mybir.dt.uint8)
    base = nc.lookup_mloc(arena).addr
    cur = [0]

    def place(name, shape, dtype, at=None):
        nbytes = int(np.prod(shape[1:])) * mybir.dt.size(dtype)
        if at is None:
            at = cur[0]
        cur[0] = max(cur[0], at + ((nbytes + 31) // 32) * 32)
        h = nc.virtual_sbuf_tensor(name, list(shape), dtype)
        nc._tensor(h.name, h.shape, h.dtype, type="SB")
        mloc = nc.lookup_mloc(h)
        mloc.allocated = True
        mloc.addr = base + at
        assert at + nbytes <= ARENA, f"{name} overflows arena ({at + nbytes})"
        return h

    Xt = place("Xt", [128, NT * KB], BF16)
    identT = place("identT", [128, 128], BF16)
    maskc = place("maskcS", [128, 384], BF16)
    onescol = place("onescolB", [128, 1], BF16)
    onesf = place("onesfR", [1, 128], F32)
    cns = place("cns", [128, 8], F32)
    qb_sb = place("qb_sb", [128, 8], F32)
    kb_sb = place("kb_sb", [128, 8], F32)
    colbias = place("colbias", [128, KB], BF16)
    colbias2 = place("colbias2", [128, KB], BF16)
    st6a = place("st6a", [128, 12], F32)
    st6b = place("st6b", [128, 12], F32)
    st6p4 = [st6a, st6b, st6a, st6b]
    mvtp2 = [place("mvt%d" % i, [128, 2], F32) for i in range(2)]
    mvtp = [mvtp2[0], mvtp2[1], mvtp2[0], mvtp2[1]]
    rrp = [place("rr%d" % i, [128, 1], F32) for i in range(NT)]
    bbp = [place("bb%d" % i, [128, 1], F32) for i in range(NT)]
    rcpt = place("rcpt", [128, 8], F32)

    R1 = cur[0]; place("R1pad", [128, 32 * KB], mybir.dt.uint8, at=R1)
    R2 = cur[0]; place("R2pad", [128, 32 * KB], mybir.dt.uint8, at=R2)
    R3 = cur[0]; place("R3pad", [128, 33 * KB], mybir.dt.uint8, at=R3)
    R4 = cur[0]; place("R4pad", [128, 36 * KB], mybir.dt.uint8, at=R4)
    R5 = cur[0]; place("R5pad", [128, 16 * KB], mybir.dt.uint8, at=R5)
    R6 = cur[0]; place("R6pad", [128, 16 * KB], mybir.dt.uint8, at=R6)
    R7 = cur[0]; place("R7pad", [128, 4 * KB], mybir.dt.uint8, at=R7)
    assert cur[0] <= ARENA, f"arena too small: {cur[0]}"

    ATr = place("ATr", [128, 8192], BF16, at=R1)
    ATi = place("ATi", [128, 8192], BF16, at=R1 + 16 * KB)
    ATboth = place("ATboth", [128, 16384], BF16, at=R1)
    QT = place("QT", [128, 16384], BF16, at=R1)
    zt = place("zt", [128, 4096], F32, at=R1)
    ut = place("ut", [128, 4096], BF16, at=R1 + 16 * KB)

    Cc = place("Cc", [128, 8192], BF16, at=R2)
    Scm = place("Scm", [128, 8192], BF16, at=R2 + 16 * KB)
    KT = place("KT", [128, 16384], BF16, at=R2)
    ostg = place("ostg", [128, 2048], F32, at=R2)

    Wrt = place("Wrt", [128, 8192], BF16, at=R3)
    Wit = place("Wit", [128, 8192], BF16, at=R3 + 16 * KB)
    Vt = place("Vt", [128, 16640], BF16, at=R3)

    tmpcs = place("tmpcs", [128, 8192], BF16, at=R4)
    hbT = place("hbT", [128, 16384], BF16, at=R4)
    aT = place("aT", [128, 16384], BF16, at=R4)
    h2T = place("h2T", [128, 16384], BF16, at=R4)
    hbtok = place("hbtok", [128, 2048], BF16, at=R4 + 32 * KB)
    h2tok = place("h2tok", [128, 2048], BF16, at=R4 + 32 * KB)

    Snm = place("Snm", [128, 8192], BF16, at=R4 + 16 * KB)
    Mt = place("Mt", [128, 8192], BF16, at=R5)
    wslotB = place("wslotB", [128, 8192], BF16, at=R5)
    PTr = place("PTr", [128, 4 * 384], BF16, at=R5)
    apair = place("apair", [128, 4096], BF16, at=R5 + 4 * KB)

    wslot = place("wslot", [128, 8192], BF16, at=R6)
    rowstg = place("rowstg", [1, 1024], F32, at=R7)
    rrp2 = [place("rr2_%d" % i, [128, 1], F32, at=R7 + i * 32) for i in range(NT)]
    bbp2 = [place("bb2_%d" % i, [128, 1], F32, at=R7 + 512 + i * 32) for i in range(NT)]

    pbig = nc.alloc_psum_tensor("pbig", [128, 3072], F32)
    ptrp = nc.alloc_psum_tensor("ptrp", [128, 2048], BF16)

    sems = {s_: nc.alloc_semaphore(name="sem_" + s_) for s_ in ("ld", "g", "t", "v", "s")}
    sc = Sched()
    dbg_gate = [0]

    def seg(name):
        sc.tag = name
        if SAFE_TAGS["tags"]:
            SAFE_MODE["on"] = (name in SAFE_TAGS["tags"]) or ("all" in SAFE_TAGS["tags"])

    def dbg_dump(name, src, waits):
        if debug_stage != name:
            return
        n = min(src.shape[1], 16384)
        m = sc.emit("gpsimd", lambda e, src=src, n=n: e.dma_start(out=ddbg[:, 0:n], in_=src[:, 0:n]),
                    waits=waits, inc="g")
        dbg_gate[0] = m

    def dma(q, out, in_, waits=(), slow=False):
        if slow:
            return sc.emit(q, lambda e, out=out, in_=in_: e.dma_start(
                out=out, in_=in_, allow_slow_non_contiguous=True), waits=waits,
                inc={"sync": "ld", "gpsimd": "g"}[q])
        return sc.emit(q, lambda e, out=out, in_=in_: e.dma_start(out=out, in_=in_),
                       waits=waits, inc={"sync": "ld", "gpsimd": "g"}[q])

    def T(fn, waits=(), inc=None):
        return sc.emit("tensor", fn, waits, inc)

    def Vx(fn, waits=(), inc=None):
        return sc.emit("vector", fn, waits, inc)

    def Sx(fn, waits=(), inc=None):
        return sc.emit("scalar", fn, waits, inc)

    bank = [pbig[:, b * 512:(b + 1) * 512] for b in range(6)]

    def mm(out_ap, lh, rh, start, stop, waits=(), inc=None):
        return T(lambda e, out_ap=out_ap, lh=lh, rh=rh, start=start, stop=stop:
                 e.matmul(out_ap, lh, rh, start=start, stop=stop, skip_group_check=True),
                 waits=waits, inc=inc)

    seg("load")
    Vx(lambda e: e.memset(cns[:, 0:1], 1e-5))
    Vx(lambda e: e.memset(cns[:, 1:2], -0.5))
    Vx(lambda e: e.memset(cns[:, 2:3], 0.125))
    Vx(lambda e: e.memset(cns[:, 3:4], 1.0 / 1024.0))
    Vx(lambda e: e.memset(cns[:, 4:5], -1.0 / 1024.0))
    Vx(lambda e: e.memset(cns[:, 5:6], -1.0))
    Vx(lambda e: e.memset(onescol[:], 1.0))
    init_v = Vx(lambda e: e.memset(onesf[:], 1.0), inc="v")

    ld_ident = dma("sync", identT[:], dident)
    ld_mask = dma("sync", maskc[:], dmask)
    ld_qb = dma("sync", qb_sb[:], db["qb"].rearrange("(c p) -> p c", p=128), slow=True)
    ld_kb = dma("sync", kb_sb[:], db["kb"].rearrange("(c p) -> p c", p=128), slow=True)

    def castdma(dst_t, src_ap, nchunk, waits=()):
        dst = dst_t[:, :].rearrange("p (c j) -> p c j", c=nchunk)
        srcv = src_ap.rearrange("(c p) j -> p c j", p=128)
        return dma("gpsimd", dst, srcv, waits=waits)

    def syncdma(dst_t, src_ap, nchunk, waits=()):
        dst = dst_t[:, :].rearrange("p (c j) -> p c j", c=nchunk)
        srcv = src_ap.rearrange("(c p) j -> p c j", p=128)
        return dma("sync", dst, srcv, waits=waits)

    syncdma(Cc, dw["c0"], NCH)
    t1 = syncdma(tmpcs, dw["c1"], NCH)
    cadd = None
    for jc in range(NCH):
        cadd = Vx(lambda e, jc=jc: e.tensor_add(
            Cc[:, jc * KB:(jc + 1) * KB], tmpcs[:, jc * KB:(jc + 1) * KB],
            Cc[:, jc * KB:(jc + 1) * KB]),
            waits=(("ld", t1), ("v", init_v)) if jc == 0 else (), inc="v")
    t2 = syncdma(tmpcs, dw["c2"], NCH, waits=(("v", cadd),))
    for jc in range(NCH):
        cadd = Vx(lambda e, jc=jc: e.tensor_add(
            Cc[:, jc * KB:(jc + 1) * KB], tmpcs[:, jc * KB:(jc + 1) * KB],
            Cc[:, jc * KB:(jc + 1) * KB]),
            waits=(("ld", t2),) if jc == 0 else (), inc="v")
    syncdma(Scm, dw["s0"], NCH, waits=(("v", cadd),))
    t3 = syncdma(tmpcs, dw["s1"], NCH)
    sadd = None
    for jc in range(NCH):
        sadd = Vx(lambda e, jc=jc: e.tensor_add(
            Scm[:, jc * KB:(jc + 1) * KB], tmpcs[:, jc * KB:(jc + 1) * KB],
            Scm[:, jc * KB:(jc + 1) * KB]),
            waits=(("ld", t3),) if jc == 0 else (), inc="v")
    t4 = syncdma(tmpcs, dw["s2"], NCH, waits=(("v", sadd),))
    for jc in range(NCH):
        sadd = Vx(lambda e, jc=jc: e.tensor_add(
            Scm[:, jc * KB:(jc + 1) * KB], tmpcs[:, jc * KB:(jc + 1) * KB],
            Scm[:, jc * KB:(jc + 1) * KB]),
            waits=(("ld", t4),) if jc == 0 else (), inc="v")
    sn_m = None
    for jc in range(NCH):
        sn_m = Vx(lambda e, jc=jc: e.tensor_scalar_mul(
            out=Snm[:, jc * KB:(jc + 1) * KB], in0=Scm[:, jc * KB:(jc + 1) * KB],
            scalar1=cns[:, 5:6]),
            waits=(("v", sadd),) if jc == 0 else (), inc="v")
    wr_g = castdma(Wrt, dw["wr"], NCH)
    wi_g = castdma(Wit, dw["wi"], NCH)
    x_g4 = []
    for xp in range(4):
        x_g4.append(dma(
            "gpsimd",
            Xt[:, xp * 4096:(xp + 1) * 4096].rearrange("p (c j) -> p c j", c=4),
            dx[xp * 512:(xp + 1) * 512, :].rearrange("(c p) j -> p c j", p=128)))
    x_g = x_g4[-1]
    qw_g = castdma(wslot, dw["qwT"], NCH)
    stats_gate = [0, 0, 0, 0]
    apply_m16 = [0] * NT
    ln_tok_gate = [0, 0]
    ptr_free = [("v", 0), ("v", 0)]

    def ln_stats(src_ap, nt, extra_waits=(), par=None, rarr=None, barr=None):
        if par is None:
            par = nt % 2
        if rarr is None:
            rarr, barr = rrp, bbp
        w = [("s", stats_gate[par])] + list(extra_waits)
        Vx(lambda e: e.bn_stats(out=st6p4[par][:, 0:6], in_=src_ap[:, 0:512]), waits=w)
        b2 = Vx(lambda e: e.bn_stats(out=st6p4[par][:, 6:12], in_=src_ap[:, 512:1024]),
                inc="v")
        mv = Vx(lambda e: e.bn_aggr(out=mvtp[par][:, :], in_=st6p4[par][:, :]),
                waits=(("v", b2),), inc="v")
        c1 = Sx(lambda e: e.activation(out=rarr[nt][:, :], in_=mvtp[par][:, 1:2],
                                       func=AF.Ln, bias=cns[:, 0:1], scale=1.0),
                waits=(("v", mv),), inc="s")
        c2 = Sx(lambda e: e.activation(out=rarr[nt][:, :], in_=rarr[nt][:, :],
                                       func=AF.Exp, bias=0.0, scale=cns[:, 1:2]),
                waits=(("s", c1),), inc="s")
        c3 = Sx(lambda e: e.mul(out=barr[nt][:, :], in_=rarr[nt][:, :],
                                mul=mvtp[par][:, 0:1]),
                waits=(("s", c2),), inc="s")
        c4 = Sx(lambda e: e.mul(out=barr[nt][:, :], in_=barr[nt][:, :], mul=cns[:, 5:6]),
                waits=(("s", c3),), inc="s")
        stats_gate[par] = c4
        return c4

    def ln_apply(src_ap, tok_out, nt, chain_m, tok_gate_t=0, rarr=None, barr=None):
        if rarr is None:
            rarr, barr = rrp, bbp
        am = Sx(lambda e, rarr=rarr, barr=barr: e.activation(
            out=tok_out, in_=src_ap, func=AF.Identity,
            bias=barr[nt][:, :], scale=rarr[nt][:, :]),
            waits=(("s", chain_m), ("t", tok_gate_t)), inc="s")
        apply_m16[nt] = am
        return am

    ident_waited = [False]

    def emit_transposes(tok_ap, dstT, nt, apply_m, extra_w=None):
        pb = nt % 2
        tm = None
        for jc in range(NCH):
            w = [("s", apply_m), ptr_free[pb]] if jc == 0 else []
            if jc == 0 and extra_w:
                w.append(extra_w)
            if jc == 0 and not ident_waited[0]:
                w.append(("ld", ld_ident))
                ident_waited[0] = True
            tm = T(lambda e, jc=jc, pb=pb, tok_ap=tok_ap: e.transpose(
                ptrp[:, pb * 1024 + jc * 128: pb * 1024 + (jc + 1) * 128],
                tok_ap[:, jc * 128:(jc + 1) * 128], identT[:]),
                waits=w, inc=("t" if jc == NCH - 1 else None))
        vm = None
        for g in range(2):
            srcv = ptrp[:, pb * 1024 + g * 512: pb * 1024 + (g + 1) * 512].rearrange(
                "p (a b) -> p a b", a=4)
            dstv = dstT[:, :].rearrange("p (j n) -> p j n", j=NCH)[
                :, 4 * g:4 * g + 4, nt * 128:(nt + 1) * 128]
            vm = Vx(lambda e, srcv=srcv, dstv=dstv: e.tensor_copy(out=dstv, in_=srcv),
                    waits=(("t", tm),) if g == 0 else (), inc=("v" if g == 1 else None))
        ptr_free[pb] = ("v", vm)
        return tm, vm


    ln1_chain = [None] * NT

    def ln1_batch(bi):
        for nt in range(bi * 4, bi * 4 + 4):
            ln1_chain[nt] = ln_stats(Xt[:, nt * KB:(nt + 1) * KB], nt,
                                     extra_waits=((("g", x_g4[bi]),) if nt % 4 == 0 else ()))
    ln1_batch(0)

    seg("p0")

    evac_free = [("s", 0)] * 6

    def accum(mms, out_ap, b, evac_fn, eng="scalar", first_waits=(), evac_waits=()):
        n = len(mms)
        tm = None
        for i, (lh, rh) in enumerate(mms):
            w = (list(first_waits) + [evac_free[b]]) if i == 0 else []
            tm = mm(out_ap, lh, rh, i == 0, i == n - 1, waits=w,
                    inc=("t" if i == n - 1 else None))
        ew = [("t", tm)] + list(evac_waits)
        if eng == "scalar":
            em = Sx(evac_fn, waits=ew, inc="s")
            evac_free[b] = ("s", em)
        else:
            em = Vx(evac_fn, waits=ew, inc="v")
            evac_free[b] = ("v", em)
        return em

    bank_it = [0]

    def nextb(ring=(0, 1, 2, 3)):
        b = ring[bank_it[0] % len(ring)]
        bank_it[0] += 1
        return b

    def wch(t, f, blk):
        return t[:, f * KB + blk * 128: f * KB + (blk + 1) * 128]

    def cch(t, f, w):
        return t[:, f * KB + w * 512: f * KB + (w + 1) * 512]

    at_evac = {}
    gidx = 0
    for dst, pA, pB, sgn in ((ATr, (Wrt, Cc), (Wit, Scm), 3), (ATi, (Wit, Cc), (Wrt, Snm), 4)):
        for oc in range(NCH):
            for dwin in range(2):
                b = nextb()
                n_mm = 2 * NCH
                tm = None
                for i in range(n_mm):
                    half = i // NCH
                    f = i % NCH
                    src_pair = pA if half == 0 else pB
                    w = []
                    if i == 0:
                        w = [evac_free[b]]
                        if gidx == 0:
                            w += [("g", wr_g), ("v", cadd)]
                    if i == NCH and gidx == 0:
                        w = [("v", sadd), ("g", wi_g)]
                    if i == NCH and gidx == 16 and dst is ATi:
                        w = [("v", sn_m)]
                    tm = mm(bank[b], wch(src_pair[0], f, oc), cch(src_pair[1], f, dwin),
                            i == 0, i == n_mm - 1, waits=w,
                            inc=("t" if i == n_mm - 1 else None))

                def ev(e, dst=dst, oc=oc, dwin=dwin, b=b, sgn=sgn):
                    return e.activation(
                        out=dst[:, oc * KB + dwin * 512: oc * KB + (dwin + 1) * 512],
                        in_=bank[b], func=AF.Identity, bias=0.0, scale=cns[:, sgn:sgn + 1])
                em = Sx(ev, waits=(("t", tm),), inc="s")
                evac_free[b] = ("s", em)
                at_evac[(dst is ATi, oc, dwin)] = em
                gidx += 1
                if gidx in (6, 12, 18):
                    ln1_batch(gidx // 6)
    at_done = sc.cnt["s"]
    at_t_done = sc.cnt["t"]
    vones_m = None
    for nt in range(NT):
        wv_ = [("t", at_t_done)] if nt == 0 else []
        Vx(lambda e, nt=nt: e.memset(
            Vt[:, nt * 1040 + 64: nt * 1040 + 64 + 15 * 65].rearrange(
                "p (h c) -> p h c", h=15)[:, :, 0:1], 1.0), waits=wv_)
        vones_m = Vx(lambda e, nt=nt: e.memset(
            Vt[:, nt * 1040 + 1039: nt * 1040 + 1040], 1.0),
            inc=("v" if nt == NT - 1 else None))

    m_evac = {}
    for dc in range(NCH):
        for tw in range(2):
            b = nextb()
            mms = [(ATr[:, oc * KB + dc * 128: oc * KB + (dc + 1) * 128], cch(Cc, oc, tw))
                   for oc in range(NCH)] + \
                  [(ATi[:, oc * KB + dc * 128: oc * KB + (dc + 1) * 128], cch(Scm, oc, tw))
                   for oc in range(NCH)]

            def ev(e, dc=dc, tw=tw, b=b):
                return e.activation(out=Mt[:, dc * KB + tw * 512: dc * KB + (tw + 1) * 512],
                                    in_=bank[b], func=AF.Identity, bias=0.0, scale=1.0)
            fw = (("s", at_done),) if (dc == 0 and tw == 0) else ()
            m_evac[(dc, tw)] = accum(mms, bank[b], b, ev, first_waits=fw)

    m_spill_m = None
    for dc in range(NCH):
        m_spill_m = dma("gpsimd", dmsp[dc * 128:(dc + 1) * 128, :], Mt[:, dc * KB:(dc + 1) * KB],
                        waits=(("s", m_evac[(dc, 1)]),))
    if debug_stage == "AT":
        sc.emit("gpsimd", lambda e: e.dma_start(out=ddbg[:, 0:8192], in_=ATr[:, :]),
                waits=(("s", at_done),), inc="g")
        dbg_gate[0] = sc.emit("gpsimd", lambda e: e.dma_start(
            out=ddbg[:, 8192:16384], in_=ATi[:, :]), waits=(), inc="g")

    def build_colbias(nm, row_wait_s=0, mm_wait_v=0, evac_wait_v=0, dst=None):
        if dst is None:
            dst = colbias
        wl = [("s", row_wait_s)] if row_wait_s else []
        mrow = dma("sync", rowstg[0:1, :], db[nm].rearrange("(a b) -> a b", a=1), waits=wl)
        out_m = []
        for wnd in range(2):
            b = 4 + wnd
            tm = mm(bank[b], onesf[:, :], rowstg[0:1, wnd * 512:(wnd + 1) * 512],
                    True, True, waits=[("ld", mrow), ("v", init_v), evac_free[b],
                                       ("v", mm_wait_v)], inc="t")
            em = Sx(lambda e, wnd=wnd, b=b, dst=dst: e.activation(
                out=dst[:, wnd * 512:(wnd + 1) * 512], in_=bank[b],
                func=AF.Identity, bias=0.0, scale=1.0),
                waits=(("t", tm), ("v", evac_wait_v)), inc="s")
            evac_free[b] = ("s", em)
            out_m.append(em)
        return out_m[-1]

    vb_ready = build_colbias("vb")

    hbT_done = None
    seg("ln1")
    for nt in range(NT):
        am = ln_apply(Xt[:, nt * KB:(nt + 1) * KB],
                      hbtok[:, (nt % 2) * 1024:(nt % 2 + 1) * 1024], nt,
                      ln1_chain[nt], tok_gate_t=ln_tok_gate[nt % 2])
        tm, vm = emit_transposes(hbtok[:, (nt % 2) * 1024:(nt % 2 + 1) * 1024], hbT, nt, am)
        ln_tok_gate[nt % 2] = tm
        hbT_done = vm
    dbg_dump("hbT", hbT, (("v", hbT_done),))

    seg("qkv")
    RING6 = (0, 1, 2, 3, 4, 5)

    def proj_featmajor(dstT, bias_sb, wsl, w_g):
        for ic in range(NCH):
            for nw in range(4):
                b = nextb(RING6)
                mms = [(wsl[:, jc * KB + ic * 128: jc * KB + (ic + 1) * 128],
                        hbT[:, jc * 2048 + nw * 512: jc * 2048 + (nw + 1) * 512])
                       for jc in range(NCH)]

                def ev(e, ic=ic, nw=nw, b=b, dstT=dstT, bias_sb=bias_sb):
                    return e.activation(out=dstT[:, ic * 2048 + nw * 512: ic * 2048 + (nw + 1) * 512],
                                        in_=bank[b], func=AF.Identity,
                                        bias=bias_sb[:, ic:ic + 1], scale=1.0)
                fw = [("v", hbT_done), ("g", w_g)] if (ic == 0 and nw == 0) else []
                if ic == 0 and nw == 0 and debug_stage == "AT" and dstT is QT:
                    fw.append(("g", dbg_gate[0]))
                ew_b = [("ld", max(ld_qb, ld_kb))] if (ic == 0 and nw == 0) else []
                accum(mms, bank[b], b, ev, first_waits=fw, evac_waits=ew_b)
        return sc.cnt["t"], sc.cnt["s"]

    q_t_done, q_s_done = proj_featmajor(QT, qb_sb, wslot, qw_g)
    kw_g = castdma(wslotB, dw["kwT"], NCH)
    k_t_done, k_s_done = proj_featmajor(KT, kb_sb, wslotB, kw_g)
    vw_g = castdma(wslot, dw["vwT"], NCH, waits=(("t", q_t_done),))

    v_t_done = None
    v_v_done = None
    for nt in range(NT):
        for iw in range(2):
            b = nextb(RING6)
            mms = [(hbT[:, jc * 2048 + nt * 128: jc * 2048 + (nt + 1) * 128],
                    wslot[:, jc * KB + iw * 512: jc * KB + (iw + 1) * 512])
                   for jc in range(NCH)]

            def ev(e, nt=nt, iw=iw, b=b):
                dstv = Vt[:, nt * 1040 + iw * 520: nt * 1040 + iw * 520 + 520].rearrange(
                    "p (h c) -> p h c", h=8)[:, :, 0:64]
                srcv = bank[b].rearrange("p (h c) -> p h c", h=8)
                cbv = colbias[:, iw * 512:(iw + 1) * 512].rearrange("p (h c) -> p h c", h=8)
                return e.tensor_tensor(out=dstv, in0=srcv, in1=cbv, op=mybir.AluOpType.add)
            fw = [("g", vw_g), ("s", vb_ready)] if (nt == 0 and iw == 0) else []
            v_v_done = accum(mms, bank[b], b, ev, eng="vector", first_waits=fw)
    v_t_done = sc.cnt["t"]
    ow_g = castdma(wslot, dw["owT"], NCH, waits=(("t", v_t_done),))
    dbg_dump("QT", QT, (("s", q_s_done),))
    dbg_dump("KT", KT, (("s", k_s_done),))
    dbg_dump("V", Vt, (("v", v_v_done),))

    seg("a2")
    NJ = [384] * 14 + [256, 128]
    SBANKS = (0, 1, 4, 5)
    exp_bank = {b: 0 for b in SBANKS}
    av_free = [("v", 0), ("v", 0)]
    aT_done = None
    qk_gate = max(q_s_done, k_s_done)
    first_score = True

    def emit_av(h, qt, em, extra_first=()):
        slot = (h * NT + qt) % 8
        pair = slot // 4
        dcol = (2 + pair) * 512 + (slot % 4) * 128
        djs = list(range(max(0, qt - 2), qt + 1))
        avm = None
        for i, dj in enumerate(djs):
            lh = PTr[:, (dj % 4) * 384 + (qt - dj) * 128: (dj % 4) * 384 + (qt - dj + 1) * 128]
            rh = Vt[:, dj * 1040 + h * 65: dj * 1040 + h * 65 + 65]
            w = ([("s", em), av_free[pair]] + list(extra_first)) if i == 0 else []
            avm = mm(pbig[:, dcol:dcol + 65], lh, rh, i == 0, i == len(djs) - 1,
                     waits=w, inc=("t" if i == len(djs) - 1 else None))
        if slot % 4 == 3:
            pbuf = (h // 2) % 2
            po = (h % 2) * 64
            vm = None
            for s4 in range(4):
                sl = pair * 4 + s4
                qj = qt - 3 + s4
                dc4 = (2 + pair) * 512 + s4 * 128
                rm = Vx(lambda e, sl=sl, dc4=dc4: e.reciprocal(
                    out=rcpt[:, sl:sl + 1], in_=pbig[:, dc4 + 64:dc4 + 65]),
                    waits=(("t", avm),) if s4 == 0 else (), inc="v")
                vm = Vx(lambda e, sl=sl, dc4=dc4, pbuf=pbuf, qj=qj, po=po: e.tensor_scalar_mul(
                    out=apair[:, pbuf * 2048 + qj * 128 + po: pbuf * 2048 + qj * 128 + po + 64],
                    in0=pbig[:, dc4:dc4 + 64], scalar1=rcpt[:, sl:sl + 1]),
                    waits=(("v", rm),), inc=("v" if s4 == 3 else None))
            av_free[pair] = ("v", vm)

    for h in range(NHEAD):
        hp = h // 2
        po = (h % 2) * 64
        pbuf = (h // 2) % 2
        ems = [None] * NT
        for j in range(NT):
            nj = NJ[j]
            sb_ = SBANKS[j % 4]
            w = [("s", exp_bank[sb_])]
            if first_score:
                w += [("s", qk_gate), ("v", v_v_done), ("ld", ld_mask)]
                first_score = False
            mm(pbig[:, sb_ * 512: sb_ * 512 + nj],
               KT[po:po + 64, hp * 2048 + j * 128: hp * 2048 + (j + 1) * 128],
               QT[po:po + 64, hp * 2048 + j * 128: hp * 2048 + j * 128 + nj],
               True, False, waits=w)
            sm = mm(pbig[:, sb_ * 512: sb_ * 512 + nj], identT[:], maskc[:, 0:nj],
                    False, True, inc="t")
            ew_ = [("t", sm)]
            if h == 0 and j == 0:
                ew_.append(("g", m_spill_m))
            ems[j] = Sx(lambda e, sb_=sb_, nj=nj, j=j: e.activation(
                out=PTr[:, (j % 4) * 384: (j % 4) * 384 + nj],
                in_=pbig[:, sb_ * 512: sb_ * 512 + nj],
                func=AF.Exp, bias=0.0, scale=cns[:, 2:3]), waits=ew_, inc="s")
            exp_bank[sb_] = ems[j]
            if j >= 1:
                ef = [("v", v_v_done)] if (h == 0 and j == 1) else []
                emit_av(h, j - 1, ems[j - 1], extra_first=ef)
        emit_av(h, NT - 1, ems[NT - 1])
        if h % 2 == 1:
            c = h // 2
            tm = None
            for qt in range(NT):
                pb = qt // 8
                w = [av_free[0], av_free[1]] if qt == 0 else []
                if qt % 8 == 0:
                    w.append(ptr_free[pb])
                tm = T(lambda e, qt=qt, pb=pb, pbuf=pbuf: e.transpose(
                    ptrp[:, (qt % 16) * 128: (qt % 16 + 1) * 128],
                    apair[:, pbuf * 2048 + qt * 128: pbuf * 2048 + (qt + 1) * 128],
                    identT[:]), waits=w, inc=("t" if qt % 8 == 7 else None))
                if qt % 8 == 7:
                    wv = [("t", tm)]
                    if debug_stage == "hbT":
                        wv.append(("g", dbg_gate[0]))
                    vm = Vx(lambda e, pb=pb, c=c: e.tensor_copy(
                        out=aT[:, c * 2048 + pb * 1024: c * 2048 + (pb + 1) * 1024],
                        in_=ptrp[:, pb * 1024:(pb + 1) * 1024]),
                        waits=wv, inc="v")
                    ptr_free[pb] = ("v", vm)
                    aT_done = vm

    seg("a3")
    ob_ready = build_colbias("ob", row_wait_s=vb_ready, mm_wait_v=aT_done,
                             evac_wait_v=v_v_done)
    fb_ready = build_colbias("fb", row_wait_s=ob_ready, dst=colbias2)
    x1_m = [None] * NT
    ln3_chain = [None] * NT
    for nt in range(NT):
        par = nt % 2
        zms = []
        for iw in range(2):
            b = nextb(RING6)
            mms = [(aT[:, dc * 2048 + nt * 128: dc * 2048 + (nt + 1) * 128],
                    wslot[:, dc * KB + iw * 512: dc * KB + (iw + 1) * 512])
                   for dc in range(NCH)]

            def ev(e, nt=nt, iw=iw, b=b, par=par):
                return e.tensor_add(zt[:, par * 1024 + iw * 512: par * 1024 + (iw + 1) * 512],
                                    bank[b], colbias[:, iw * 512:(iw + 1) * 512])
            fw = []
            if nt == 0 and iw == 0:
                fw = [("g", ow_g), ("v", aT_done), ("s", ob_ready)]
            ew = [("s", apply_m16[nt - 2] if nt >= 2 else 0)] if iw == 0 else []
            if nt == 0 and iw == 0 and debug_stage in ("QT", "KT"):
                ew.append(("g", dbg_gate[0]))
            zms.append(accum(mms, bank[b], b, ev, eng="vector", first_waits=fw,
                             evac_waits=ew))
        cm4 = ln_stats(zt[:, par * 1024:(par + 1) * 1024], nt,
                       extra_waits=(("v", zms[-1]),), par=par)
        am = ln_apply(zt[:, par * 1024:(par + 1) * 1024],
                      ut[:, par * 1024:(par + 1) * 1024], nt, cm4)
        xm = Vx(lambda e, nt=nt, par=par: e.tensor_add(
            Xt[:, nt * KB:(nt + 1) * KB], Xt[:, nt * KB:(nt + 1) * KB],
            ut[:, par * 1024:(par + 1) * 1024]), waits=(("s", am),), inc="v")
        x1_m[nt] = xm
        ew3 = [("v", xm), ("s", fb_ready)]
        ln3_chain[nt] = ln_stats(Xt[:, nt * KB:(nt + 1) * KB], nt,
                                 extra_waits=ew3, par=nt % 2,
                                 rarr=rrp2, barr=bbp2)
    a3_t_done = sc.cnt["t"]

    seg("f")
    m_rd = [None] * NCH
    for dc in range(NCH):
        w = [("t", a3_t_done), ("g", m_spill_m)] if dc == 0 else []
        m_rd[dc] = dma("sync", Mt[:, dc * KB:(dc + 1) * KB], dmsp[dc * 128:(dc + 1) * 128, :],
                       waits=w)
    dbg_dump("aT", aT, (("v", aT_done),))
    dbg_dump("x1", Xt, (("v", x1_m[-1]),))
    h2_done = None
    for nt in range(NT):
        gw = [("g", dbg_gate[0])] if (nt == 0 and debug_stage in ("aT", "x1")) else []
        am = ln_apply(Xt[:, nt * KB:(nt + 1) * KB],
                      h2tok[:, (nt % 2) * 1024:(nt % 2 + 1) * 1024], nt,
                      ln3_chain[nt], tok_gate_t=ln_tok_gate[nt % 2],
                      rarr=rrp2, barr=bbp2)
        tm, vm = emit_transposes(h2tok[:, (nt % 2) * 1024:(nt % 2 + 1) * 1024], h2T, nt, am,
                                 extra_w=(gw[0] if gw else None))
        ln_tok_gate[nt % 2] = tm
        h2_done = vm

    x2_m = [None] * NT
    for nt in range(NT):
        par = nt % 2
        yms = []
        for tw in range(2):
            b = nextb(RING6)
            mms = [(h2T[:, dc * 2048 + nt * 128: dc * 2048 + (nt + 1) * 128],
                    Mt[:, dc * KB + tw * 512: dc * KB + (tw + 1) * 512])
                   for dc in range(NCH)]

            def ev(e, nt=nt, tw=tw, b=b, par=par):
                return e.tensor_add(zt[:, par * 1024 + tw * 512: par * 1024 + (tw + 1) * 512],
                                    bank[b], colbias2[:, tw * 512:(tw + 1) * 512])
            fw = []
            if nt == 0 and tw == 0:
                fw = [("v", h2_done), ("ld", m_rd[-1]), ("s", fb_ready)]
            ew = [("s", apply_m16[nt - 2] if nt >= 2 else 0)] if tw == 0 else []
            yms.append(accum(mms, bank[b], b, ev, eng="vector", first_waits=fw,
                             evac_waits=ew))
        cm4 = ln_stats(zt[:, par * 1024:(par + 1) * 1024], nt,
                       extra_waits=(("v", yms[-1]),), par=par)
        am = ln_apply(zt[:, par * 1024:(par + 1) * 1024],
                      ut[:, par * 1024:(par + 1) * 1024], nt, cm4)
        xm = Vx(lambda e, nt=nt, par=par: e.tensor_add(
            Xt[:, nt * KB:(nt + 1) * KB], Xt[:, nt * KB:(nt + 1) * KB],
            ut[:, par * 1024:(par + 1) * 1024]), waits=(("s", am),), inc="v")
        x2_m[nt] = xm

    seg("outp")
    st_m = [0] * NT
    for nt in range(NT):
        par = nt % 2
        w = [("v", x2_m[nt])]
        if nt >= 2:
            w.append(("g", st_m[nt - 2]))
        cm = Sx(lambda e, nt=nt, par=par: e.copy(
            out=ostg[:, par * 1024:(par + 1) * 1024], in_=Xt[:, nt * KB:(nt + 1) * KB]),
            waits=w, inc="s")
        st_m[nt] = dma("gpsimd", dout[nt * 128:(nt + 1) * 128, :],
                       ostg[:, par * 1024:(par + 1) * 1024], waits=(("s", cm),))

    if debug_stage == "M":
        dma("gpsimd", ddbg[:, 0:8192], Mt[:, 0:8192],
            waits=(("v", sc.cnt["v"]), ("s", sc.cnt["s"]), ("t", sc.cnt["t"])))

    sc.emit("gpsimd", lambda e: e.engine_nop(), waits=(("g", sc.cnt["g"]),))

    with nc.Block() as block:
        sc.build(block, sems)
    return nc


def _host_constants():
    n = np.arange(D, dtype=np.float64)
    ang = (2.0 * np.pi / D) * np.outer(n, n)
    c64, s64 = np.cos(ang), np.sin(ang)

    def split3(v):
        t0 = v.astype(ml_dtypes.bfloat16)
        r = v - t0.astype(np.float64)
        t1 = r.astype(ml_dtypes.bfloat16)
        r2 = r - t1.astype(np.float64)
        t2 = r2.astype(ml_dtypes.bfloat16)
        return t0, t1, t2
    c0, c1, c2 = split3(c64)
    s0, s1, s2 = split3(s64)
    ident = np.eye(128, dtype=ml_dtypes.bfloat16)
    kk = np.arange(128)[:, None]
    qq = np.arange(384)[None, :]
    dqk = qq - kk
    mask = np.where((dqk >= 0) & (dqk <= 255), 0.0, -30000.0).astype(ml_dtypes.bfloat16)
    return c0, c1, c2, s0, s1, s2, ident, mask


_NC_CACHE = {}


def kernel(L0_attn_x, L0_attn_q_w, L0_attn_q_b, L0_attn_k_w, L0_attn_k_b,
           L0_attn_v_w, L0_attn_v_b, L0_attn_o_w, L0_attn_o_b,
           L0_ffn_w_freq_real, L0_ffn_w_freq_imag, L0_ffn_bias):
    c0, c1, c2, s0, s1, s2, ident, mask = _host_constants()
    shared = {
        "qwT": np.ascontiguousarray(np.asarray(L0_attn_q_w, np.float32).T),
        "kwT": np.ascontiguousarray(np.asarray(L0_attn_k_w, np.float32).T),
        "vwT": np.ascontiguousarray(np.asarray(L0_attn_v_w, np.float32).T),
        "owT": np.ascontiguousarray(np.asarray(L0_attn_o_w, np.float32).T),
        "wr": np.ascontiguousarray(np.asarray(L0_ffn_w_freq_real, np.float32)),
        "wi": np.ascontiguousarray(np.asarray(L0_ffn_w_freq_imag, np.float32)),
        "c0": c0, "c1": c1, "c2": c2, "s0": s0, "s1": s1, "s2": s2,
        "qb": np.asarray(L0_attn_q_b, np.float32),
        "kb": np.asarray(L0_attn_k_b, np.float32),
        "vb": np.asarray(L0_attn_v_b, np.float32),
        "ob": np.asarray(L0_attn_o_b, np.float32),
        "fb": np.asarray(L0_ffn_bias, np.float32),
        "identin": ident, "maskin": mask,
    }
    x = np.asarray(L0_attn_x, np.float32)
    stage = DEBUG.get("stage")
    key = ("prog", stage)
    if key not in _NC_CACHE:
        _NC_CACHE[key] = build_program(stage)
    nc = _NC_CACHE[key]
    in_maps = []
    for b in range(8):
        m = dict(shared)
        m["x"] = np.ascontiguousarray(x[b])
        in_maps.append(m)
    res = run_bass_kernel_spmd(nc, in_maps, core_ids=list(range(8)))
    out = np.stack([np.asarray(r["out"], np.float32) for r in res.results], axis=0)
    if stage is not None:
        DEBUG["result"] = [np.asarray(r["dbg"]) for r in res.results]
    return out
